# version 14
# baseline (speedup 1.0000x reference)
import numpy as np
import ml_dtypes
from contextlib import ExitStack

import concourse.bass as bass
import concourse.mybir as mybir
from concourse.bass_utils import run_bass_kernel_spmd

F32 = mybir.dt.float32
BF16 = mybir.dt.bfloat16
AF = mybir.ActivationFunctionType
ALU = mybir.AluOpType
AX = mybir.AxisListType

NCORE = 8
H = 4096
S = 8192
D = 128
SLOC = S // NCORE
IPC = 11008 // NCORE
NIC = 11
ICW = [128] * 10 + [96]
ICOFF = [128 * i for i in range(11)]
NSB = 16
NKC = 64
EPS = 1e-5
NW_FULL = 11008 * 4096
RSQ = 0.08838834764831845
RG = [list(range(NCORE))]


def build(taps=()):
    nc = bass.Bass("TRN2", target_bir_lowering=False)

    x_ln = nc.dram_tensor("x_ln", [SLOC, H], F32, kind="ExternalInput").ap()
    x_res = nc.dram_tensor("x_res", [SLOC, H], F32, kind="ExternalInput").ap()
    wq4 = nc.dram_tensor("wq4", [H, 512], F32, kind="ExternalInput").ap()
    wk1 = nc.dram_tensor("wk1", [H, D], F32, kind="ExternalInput").ap()
    wv1 = nc.dram_tensor("wv1", [H, D], F32, kind="ExternalInput").ap()
    wo1 = nc.dram_tensor("wo1", [D, H], F32, kind="ExternalInput").ap()
    wgT = nc.dram_tensor("wgT", [H, IPC], F32, kind="ExternalInput").ap()
    wuT = nc.dram_tensor("wuT", [H, IPC], F32, kind="ExternalInput").ap()
    wdT = nc.dram_tensor("wdT", [IPC, H], F32, kind="ExternalInput").ap()
    out_c = nc.dram_tensor("out_c", [SLOC, H], F32, kind="ExternalOutput").ap()

    ag_in = nc.dram_tensor("ag_in", [H, SLOC], BF16).ap()
    ag_o1 = nc.dram_tensor("ag_o1", [NCORE * 2048, SLOC], BF16, addr_space="Shared").ap()
    ag_o2 = nc.dram_tensor("ag_o2", [NCORE * 2048, SLOC], BF16, addr_space="Shared").ap()
    ar_in = nc.dram_tensor("ar_in", [128, 3], F32).ap()
    ar_o = nc.dram_tensor("ar_o", [128, 3], F32).ap()
    btb_d = nc.dram_tensor("btb_d", [1, 8], F32).ap()
    qg_scr = nc.dram_tensor("qg_scr", [NIC, 128, 4096], BF16).ap()
    qu_scr = nc.dram_tensor("qu_scr", [NIC, 128, 4096], BF16).ap()
    qd_scr = nc.dram_tensor("qd_scr", [NIC, 128, 4096], BF16).ap()
    rs_in = nc.dram_tensor("rs_in", [8, 1024, H], BF16).ap()
    rs_o = nc.dram_tensor("rs_o", [8, 128, H], BF16).ap()

    tap_aps = {}
    tap_shapes = {
        "qt": ([D, S], BF16), "kt": ([D, S], BF16), "vs": ([128, S], BF16),
        "dn": ([1, S], F32), "valn": ([D, S], BF16),
        "interT": ([128, NIC * 512], BF16), "rsin0": ([1024, H], F32),
        "wdtq": ([128, NIC * 4096], BF16), "btbl": ([128, 8], F32),
        "agin": ([H, SLOC], BF16),
    }
    for t in taps:
        sh, dt = tap_shapes[t]
        tap_aps[t] = nc.dram_tensor("tap_" + t, sh, dt, kind="ExternalOutput").ap()

    PROG = []
    CNT = {}
    SEMS = {}

    def em(eng, fn):
        PROG.append((eng, fn))

    def inc(sem, n):
        CNT[sem] = CNT.get(sem, 0) + n
        return CNT[sem]

    def dma(q, out, in_, sem, **kw):
        t = inc(sem, 16)
        em(q, lambda e, o=out, i=in_, kw=dict(kw), s=sem: e.dma_start(out=o, in_=i, **kw).then_inc(SEMS[s], 16))
        return t

    def op(eng, fn, sem=None):
        if sem is None:
            em(eng, fn)
            return None
        t = inc(sem, 1)
        em(eng, lambda e, f=fn, s=sem: f(e).then_inc(SEMS[s], 1))
        return t

    DMA_SEMS = {"lb0", "lb1", "qlb0", "qlb1", "agw0", "agw1", "ard", "qs0", "qs1", "wkv", "wdl", "xq0",
                "xq1", "gwa", "gwb", "gwc", "xns", "ws0", "ws1", "fd0", "fd1",
                "ff0", "ff1", "st"}

    def wait(eng, sem, val):
        if not val:
            return
        assert val <= CNT.get(sem, 0), f"wait on future count {sem}={val} > {CNT.get(sem, 0)}"
        if sem in DMA_SEMS:
            assert val == CNT[sem], f"non-prefix DMA wait {sem}: {val} != {CNT[sem]}"
        em(eng, lambda e, s=sem, v=val: e.wait_ge(SEMS[s], v))

    def dr(eng):
        em(eng, lambda e: e.drain())

    def coll(kind, alu, ins, outs, sem):
        t = inc(sem, 1)
        em("gpsimd", lambda e, k=kind, a=alu, i=ins, o=outs, s=sem:
           e.collective_compute(k, a, replica_groups=RG, ins=i, outs=o).then_inc(SEMS[s], 1))
        return t

    with ExitStack() as top:
        ec = top.enter_context
        for sn in (["ld", "gd", "pe", "sc", "ve", "gp", "st", "ag", "ar",
                    "lb0", "lb1", "agw0", "agw1", "ard", "qs0", "qs1", "wkv",
                    "xq0", "xq1", "gwa", "gwb", "gwc", "xns", "wdl", "ws0", "ws1", "qlb0", "qlb1",
                    "fd0", "fd1", "ff0", "ff1"] + [f"rs{m}" for m in range(8)]):
            SEMS[sn] = ec(nc.semaphore(sn))

        ident = ec(nc.sbuf_tensor("ident", [128, 128], BF16))
        ones_a = ec(nc.sbuf_tensor("ones_a", [128, 1], F32))
        ones_ab = ec(nc.sbuf_tensor("ones_ab", [128, 128], BF16))
        ones_b = ec(nc.sbuf_tensor("ones_b", [1, 128], F32))
        btbl = ec(nc.sbuf_tensor("btbl", [128, 8], F32))
        sc_t = ec(nc.sbuf_tensor("sc_t", [1, 16], F32))
        wo_sl = ec(nc.sbuf_tensor("wo_sl", [128, H], BF16))

        ptr_scope = ExitStack()
        ptr = [ptr_scope.enter_context(nc.psum_tensor(f"ptr{i}", [128, 1024], BF16)) for i in range(2)]
        qkv_scope = ExitStack()
        qt_s = qkv_scope.enter_context(nc.sbuf_tensor("qt_s", [128, S], BF16, side="right"))
        kt_s = qkv_scope.enter_context(nc.sbuf_tensor("kt_s", [128, S], BF16, side="right"))
        v_s = qkv_scope.enter_context(nc.sbuf_tensor("v_s", [128, S], BF16, side="right"))
        vst = qkv_scope.enter_context(nc.sbuf_tensor("vst", [128, 512], BF16, side="right"))
        stat_scope = ExitStack()
        wq_s = stat_scope.enter_context(nc.sbuf_tensor("wq_s", [128, 4096], BF16, side="right"))
        xnq = [stat_scope.enter_context(nc.sbuf_tensor(f"xnq{i}", [128, 2048], BF16, side="right")) for i in range(2)]
        wk_s = stat_scope.enter_context(nc.sbuf_tensor("wk_s", [128, 4096], BF16, side="right"))
        wv_s = stat_scope.enter_context(nc.sbuf_tensor("wv_s", [128, 4096], BF16, side="right"))

        p0 = ExitStack()
        lbuf = [p0.enter_context(nc.sbuf_tensor(f"lbuf{i}", [128, 4096], F32)) for i in range(2)]
        xnb = [p0.enter_context(nc.sbuf_tensor(f"xnb{i}", [128, 4096], BF16)) for i in range(2)]
        snk = p0.enter_context(nc.sbuf_tensor("snk", [128, 4096], BF16))
        agst = [p0.enter_context(nc.sbuf_tensor("agst0", [128, 4096], BF16))] * 2
        st_m = p0.enter_context(nc.sbuf_tensor("st_m", [128, 6], F32))

        wk_ld = dma("gpsimd", wk_s.ap().rearrange("p (hc d) -> p hc d", d=128),
                    wk1.rearrange("(hc p) d -> p hc d", p=128), "wkv")
        wv_ld = dma("gpsimd", wv_s.ap().rearrange("p (hc d) -> p hc d", d=128),
                    wv1.rearrange("(hc p) d -> p hc d", p=128), "wkv")
        wo_ld = dma("gpsimd", wo_sl[:, :], wo1[:, :], "wkv")
        op("gpsimd", lambda e: e.memset(ident[:, :], 0.0))
        dr("gpsimd")
        ident_gp = op("gpsimd", lambda e: e.affine_select(
            out=ident[:, :], in_=ident[:, :], compare_op=ALU.not_equal,
            fill=1.0, base=0, pattern=[[-1, 128]], channel_multiplier=1), sem="gp")
        op("vector", lambda e: e.memset(ones_a[:, :], 1.0))
        op("vector", lambda e: e.memset(ones_ab[:, :], 1.0))
        op("vector", lambda e: e.memset(ones_b[:, :], 1.0), sem="ve")

        ln_done_ve = [0] * 8
        tr_done_pe = [0] * 8
        ag_dma_done = [0] * 8
        cp_sc = {}

        for t in range(8):
            b = t % 2
            if t >= 2:
                wait("sync", "ve", ln_done_ve[t - 2])
            ld_t = dma("sync", lbuf[b][:, :], x_ln[t * 128:(t + 1) * 128, :], f"lb{b}")
            wait("vector", f"lb{b}", ld_t)
            op("vector", lambda e, b=b: e.reduce_sum(out=st_m[:, 0:1], in_=lbuf[b][:, :], axis=AX.X), sem="ve")
            dr("vector")
            wait("scalar", f"lb{b}", ld_t)
            if t >= 1:
                wait("scalar", "ve", ln_done_ve[t - 1])
            sq_t = op("scalar", lambda e, b=b: e.activation(
                out=snk[:, :], in_=lbuf[b][:, :], func=AF.Square,
                accum_out=st_m[:, 1:2]), sem="sc")
            wait("vector", "sc", sq_t)
            op("vector", lambda e: e.tensor_scalar(out=st_m[:, 2:3], in0=st_m[:, 0:1],
                                                   scalar1=1.0 / H, scalar2=None, op0=ALU.mult))
            dr("vector")
            op("vector", lambda e: e.tensor_tensor(out=st_m[:, 3:4], in0=st_m[:, 2:3], in1=st_m[:, 2:3], op=ALU.mult))
            op("vector", lambda e: e.tensor_scalar(out=st_m[:, 4:5], in0=st_m[:, 1:2],
                                                   scalar1=1.0 / H, scalar2=None, op0=ALU.mult))
            dr("vector")
            op("vector", lambda e: e.tensor_tensor(out=st_m[:, 4:5], in0=st_m[:, 4:5], in1=st_m[:, 3:4], op=ALU.subtract))
            dr("vector")
            op("vector", lambda e: e.tensor_scalar(out=st_m[:, 4:5], in0=st_m[:, 4:5],
                                                   scalar1=EPS, scalar2=None, op0=ALU.add))
            dr("vector")
            rc_t = op("vector", lambda e: e.reciprocal(out=st_m[:, 5:6], in_=st_m[:, 4:5]), sem="ve")
            wait("scalar", "ve", rc_t)
            rs_t = op("scalar", lambda e: e.activation(out=st_m[:, 5:6], in_=st_m[:, 5:6], func=AF.Sqrt), sem="sc")
            wait("vector", "sc", rs_t)
            op("vector", lambda e: e.tensor_tensor(out=st_m[:, 3:4], in0=st_m[:, 2:3], in1=st_m[:, 5:6], op=ALU.mult))
            dr("vector")
            op("vector", lambda e: e.tensor_scalar(out=st_m[:, 3:4], in0=st_m[:, 3:4],
                                                   scalar1=-1.0, scalar2=None, op0=ALU.mult))
            dr("vector")
            if t >= 2:
                wait("vector", "pe", tr_done_pe[t - 2])
            op("vector", lambda e, b=b: e.tensor_scalar(
                out=xnb[b][:, :], in0=lbuf[b][:, :],
                scalar1=st_m[:, 5:6], scalar2=st_m[:, 3:4], op0=ALU.mult, op1=ALU.add), sem="ve")
            ln_done_ve[t] = CNT["ve"]

            wait("tensor", "ve", ln_done_ve[t])
            if t == 0:
                wait("tensor", "gp", ident_gp)
            if t >= 1:
                wait("scalar", f"agw{(t - 1) % 2}", ag_dma_done[t - 1])
            for g in range(8):
                pp2 = ptr[g % 2]
                prev = cp_sc.get((t, g - 2), cp_sc.get((t - 1, g + 6)))
                wait("tensor", "sc", prev)
                for j in range(4):
                    hc = g * 4 + j
                    op("tensor", lambda e, b=b, hc=hc, pp2=pp2, j=j: e.transpose(
                        pp2[:, j * 128:(j + 1) * 128], xnb[b][:, hc * 128:(hc + 1) * 128], ident[:, :]),
                       sem="pe" if j == 3 else None)
                tg = CNT["pe"]
                wait("scalar", "pe", tg)
                cp_sc[(t, g)] = op("scalar", lambda e, b=b, g=g, pp2=pp2: e.copy(
                    out=agst[b][:, g * 512:(g + 1) * 512], in_=pp2[:, 0:512]), sem="sc")
            tr_done_pe[t] = CNT["pe"]
            wait("gpsimd", "sc", cp_sc[(t, 7)])
            ag_view = ag_in[:, t * 128:(t + 1) * 128].rearrange("(hc p) j -> p hc j", p=128)
            agr = agst[b].rearrange("p (hc j) -> p hc j", j=128)
            ag_dma_done[t] = dma("gpsimd", ag_view, agr, f"agw{t % 2}")

        wait("gpsimd", "agw0", ag_dma_done[6])
        wait("gpsimd", "agw1", ag_dma_done[7])
        ag_h1 = coll("AllGather", ALU.bypass, [ag_in[0:2048, :].opt()], [ag_o1[:, :].opt()], "ag")
        ag_done = coll("AllGather", ALU.bypass, [ag_in[2048:4096, :].opt()], [ag_o2[:, :].opt()], "ag")
        if "agin" in tap_aps:
            dma("gpsimd", tap_aps["agin"], ag_in[:, :], "st")

        lb_free = {0: ln_done_ve[6], 1: ln_done_ve[7]}
        for hc in range(32):
            b = hc % 2
            wait("sync", "ve", lb_free[b])
            ldt = dma("sync", lbuf[b][:, 0:512], wq4[hc * 128:(hc + 1) * 128, :], f"lb{b}")
            wait("vector", f"lb{b}", ldt)
            bb = lbuf[b]
            op("vector", lambda e, bb=bb: e.tensor_tensor(out=bb[:, 0:128], in0=bb[:, 0:128], in1=bb[:, 128:256], op=ALU.add))
            op("vector", lambda e, bb=bb: e.tensor_tensor(out=bb[:, 256:384], in0=bb[:, 256:384], in1=bb[:, 384:512], op=ALU.add))
            dr("vector")
            op("vector", lambda e, bb=bb: e.tensor_tensor(out=bb[:, 0:128], in0=bb[:, 0:128], in1=bb[:, 256:384], op=ALU.add))
            dr("vector")
            lb_free[b] = op("vector", lambda e, bb=bb, hc=hc: e.tensor_scalar(
                out=wq_s[:, hc * 128:(hc + 1) * 128], in0=bb[:, 0:128],
                scalar1=RSQ, scalar2=None, op0=ALU.mult), sem="ve")
        wq_done = CNT["ve"]

        acc = p0.enter_context(nc.sbuf_tensor("acc", [128, 4], F32))
        tmp1 = p0.enter_context(nc.sbuf_tensor("tmp1", [128, 2], F32))
        tmpb = p0.enter_context(nc.sbuf_tensor("tmpb", [128, 4096], BF16))
        qout = [p0.enter_context(nc.sbuf_tensor(f"qout{i}", [128, 4096], BF16)) for i in range(2)]

        op("vector", lambda e: e.memset(acc[:, :], 0.0), sem="ve")
        srcs = [(wgT, 0), (wuT, 1), (wdT, 2)]
        k = 0
        for (src, col) in srcs:
            if col < 2:
                tiles = [(src[n * 128:(n + 1) * 128, :], 128, IPC) for n in range(32)]
            else:
                tiles = [(src[ICOFF[n]:ICOFF[n] + ICW[n], :], ICW[n], 4096) for n in range(NIC)]
            for (tap, rows, cols) in tiles:
                b = k % 2
                qn, sname = ("sync", "lb0") if b == 0 else ("gpsimd", "qlb1")
                wait(qn, "ve", lb_free[b])
                ldt = dma(qn, lbuf[b][0:rows, 0:cols], tap, sname)
                wait("vector", sname, ldt)
                op("vector", lambda e, rows=rows, cols=cols, b=b: e.reduce_sum(
                    out=tmp1[0:rows, 0:1], in_=lbuf[b][0:rows, 0:cols], axis=AX.X,
                    apply_absolute_value=True))
                dr("vector")
                lb_free[b] = op("vector", lambda e, rows=rows, col=col: e.tensor_tensor(
                    out=acc[0:rows, col:col + 1], in0=acc[0:rows, col:col + 1],
                    in1=tmp1[0:rows, 0:1], op=ALU.add), sem="ve")
                dr("vector")
                k += 1

        wait("gpsimd", "ve", CNT["ve"])
        ar_w = dma("gpsimd", ar_in[:, :], acc[:, 0:3], "ard")
        wait("gpsimd", "ard", ar_w)
        ar_done = coll("AllReduce", ALU.add, [ar_in[:, :].opt()], [ar_o[:, :].opt()], "ar")
        wait("gpsimd", "ar", ar_done)
        ar_r = dma("gpsimd", acc[:, 0:3], ar_o[:, :], "ard")

        wait("gpsimd", "ard", ar_r)
        gp_sum = op("gpsimd", lambda e: e.tensor_reduce(
            out=sc_t[0:1, 0:3], in_=acc[:, 0:3], axis=AX.C, op=ALU.add), sem="gp")
        wait("vector", "gp", gp_sum)
        dr("vector")
        op("vector", lambda e: e.tensor_scalar(out=sc_t[0:1, 4:7], in0=sc_t[0:1, 0:3],
                                               scalar1=0.5 / NW_FULL, scalar2=EPS / 2, op0=ALU.mult, op1=ALU.add))
        dr("vector")
        op("vector", lambda e: e.tensor_scalar(out=sc_t[0:1, 8:11], in0=sc_t[0:1, 4:7],
                                               scalar1=-1.0, scalar2=None, op0=ALU.mult))
        dr("vector")
        op("vector", lambda e: e.tensor_tensor(out=sc_t[0:1, 12:13], in0=sc_t[0:1, 0:1], in1=sc_t[0:1, 1:2], op=ALU.mult))
        dr("vector")
        op("vector", lambda e: e.tensor_tensor(out=sc_t[0:1, 12:13], in0=sc_t[0:1, 12:13], in1=sc_t[0:1, 2:3], op=ALU.mult))
        dr("vector")
        op("vector", lambda e: e.tensor_scalar(out=sc_t[0:1, 13:14], in0=sc_t[0:1, 12:13],
                                               scalar1=(1.0 / NW_FULL) ** 3, scalar2=None, op0=ALU.mult))
        dr("vector")
        op("vector", lambda e: e.tensor_copy(out=sc_t[0:1, 0:3], in_=sc_t[0:1, 4:7]))
        dr("vector")
        op("vector", lambda e: e.tensor_copy(out=sc_t[0:1, 3:6], in_=sc_t[0:1, 8:11]))
        dr("vector")
        op("vector", lambda e: e.tensor_copy(out=sc_t[0:1, 6:7], in_=sc_t[0:1, 13:14]))
        dr("vector")
        tbl_ve = op("vector", lambda e: e.memset(sc_t[0:1, 7:8], 0.0), sem="ve")
        wait("gpsimd", "ve", tbl_ve)
        bw_t = dma("gpsimd", btb_d[:, :], sc_t[0:1, 0:8], "ard")
        wait("gpsimd", "ard", bw_t)
        btbl_ard = dma("gpsimd", btbl[:, :], btb_d[0:1, :].partition_broadcast(128), "ard")
        if "btbl" in tap_aps:
            wait("sync", "ard", btbl_ard)
            dma("sync", tap_aps["btbl"], btbl[:, :], "st")


        pp = ExitStack()
        pq = pp.enter_context(nc.psum_tensor("pq", [128, 512], F32))
        pk = pp.enter_context(nc.psum_tensor("pk", [128, 512], F32))
        pv = pp.enter_context(nc.psum_tensor("pv", [128, 512], F32))

        wait("scalar", "ag", ag_h1)

        wait("tensor", "ve", wq_done)
        wait("tensor", "wkv", wo_ld)
        xnq_pe = {0: 0, 1: 0}
        vtr_ve = {0: 0, 1: 0}
        evac_sc = 0
        vst_pe = 0
        for sb in range(NSB):
            r, cs = sb // 2, (sb % 2) * 512
            for hg in range(8):
                bx = hg % 2
                if hg < 4:
                    src = ag_o1[r * 2048 + hg * 512: r * 2048 + (hg + 1) * 512, cs:cs + 512]
                else:
                    src = ag_o2[r * 2048 + (hg - 4) * 512: r * 2048 + (hg - 3) * 512, cs:cs + 512]
                src = src.rearrange("(hc p) q -> p hc q", p=128)
                if sb == 0 and hg == 4:
                    wait("scalar", "ag", ag_done)
                wait("scalar", "pe", xnq_pe[bx])
                ldt = dma("scalar", xnq[bx].ap().rearrange("p (hc q) -> p hc q", q=512), src, f"xq{bx}")
                wait("tensor", f"xq{bx}", ldt)
                if hg == 0 and sb > 0:
                    wait("tensor", "sc", evac_sc)
                for j in range(4):
                    hc = hg * 4 + j
                    st0, sp0 = (hc == 0), (hc == 31)
                    rhs = xnq[bx][:, j * 512:(j + 1) * 512]
                    op("tensor", lambda e, hc=hc, rhs=rhs, st0=st0, sp0=sp0: e.matmul(
                        pq[:, :], wq_s[:, hc * 128:(hc + 1) * 128], rhs, start=st0, stop=sp0))
                    op("tensor", lambda e, hc=hc, rhs=rhs, st0=st0, sp0=sp0: e.matmul(
                        pk[:, :], wk_s[:, hc * 128:(hc + 1) * 128], rhs, start=st0, stop=sp0))
                    op("tensor", lambda e, hc=hc, rhs=rhs, st0=st0, sp0=sp0: e.matmul(
                        pv[:, :], wv_s[:, hc * 128:(hc + 1) * 128], rhs, start=st0, stop=sp0),
                       sem="pe" if j == 3 else None)
                xnq_pe[bx] = CNT["pe"]
            mm_last = CNT["pe"]
            wait("scalar", "pe", mm_last)
            op("scalar", lambda e, sb=sb: e.copy(out=qt_s[:, sb * 512:(sb + 1) * 512], in_=pq[:, :]))
            op("scalar", lambda e, sb=sb: e.copy(out=kt_s[:, sb * 512:(sb + 1) * 512], in_=pk[:, :]))
            wait("scalar", "pe", vst_pe)
            evac_sc = op("scalar", lambda e: e.copy(out=vst[:, :], in_=pv[:, :]), sem="sc")
            wait("tensor", "sc", evac_sc)
            for j in range(4):
                bp = j % 2
                wait("tensor", "sc", vtr_ve[bp])
                tj = op("tensor", lambda e, j=j, pp2=ptr[bp]: e.transpose(
                    pp2[:, 0:128], vst[:, j * 128:(j + 1) * 128], ident[:, :]), sem="pe")
                wait("scalar", "pe", tj)
                vtr_ve[bp] = op("scalar", lambda e, sb=sb, j=j, pp2=ptr[bp]: e.copy(
                    out=v_s[:, (sb * 4 + j) * 128:(sb * 4 + j + 1) * 128], in_=pp2[:, 0:128]), sem="sc")
            vst_pe = CNT["pe"]

        pp_sc = CNT["sc"]
        pp_ve = CNT["ve"]
        pp_pe = CNT["pe"]
        if "qt" in tap_aps:
            wait("sync", "sc", pp_sc)
            dma("sync", tap_aps["qt"], qt_s[:, :], "st")
        if "kt" in tap_aps:
            wait("sync", "sc", pp_sc)
            dma("sync", tap_aps["kt"], kt_s[:, :], "st")
        if "vs" in tap_aps:
            wait("sync", "ve", pp_ve)
            dma("sync", tap_aps["vs"], v_s[:, :], "st")
        pp.close()
        stat_scope.close()
        ptr_scope.close()

        tmpa = [xnb[0], xnb[1]]
        q_gd = {0: 0, 1: 0}
        q_ve = {0: 0, 1: 0}
        qtiles = []
        for (src_, col) in srcs:
            for icn in range(NIC):
                qtiles.append((src_, col, icn))
        qstate = {"kq": 0}

        def emit_quant_tile():
            kq = qstate["kq"]
            if kq >= len(qtiles):
                return
            qstate["kq"] = kq + 1
            src_, col, icn = qtiles[kq]
            w = ICW[icn]
            if col < 2:
                tap = src_[:, ICOFF[icn]:ICOFF[icn] + w].rearrange("(hc p) i -> p hc i", p=128)
                rows, cols = 128, 32 * w
            else:
                tap = src_[ICOFF[icn]:ICOFF[icn] + w, :]
                rows, cols = w, 4096
            b = kq % 2
            qn, sname = ("sync", "lb0") if b == 0 else ("gpsimd", "qlb1")
            wait(qn, "ve", lb_free[b])
            if col < 2:
                ldt = dma(qn, lbuf[b][0:rows, 0:cols].rearrange("p (hc i) -> p hc i", i=w), tap, sname)
            else:
                ldt = dma(qn, lbuf[b][0:rows, 0:cols], tap, sname)
            wait("vector", sname, ldt)
            if kq == 0:
                wait("vector", "ard", btbl_ard)
                wait("vector", "pe", tr_done_pe[7])
            ha = tmpa[b][0:rows, 0:cols]
            dst = qout[b][0:rows, 0:cols]
            wait("vector", f"qs{b}", q_gd[b])
            dr("vector")
            op("vector", lambda e, rows=rows, cols=cols, col=col, b=b, ha=ha: e.tensor_scalar(
                out=ha, in0=lbuf[b][0:rows, 0:cols],
                scalar1=btbl[0:rows, 3 + col:4 + col], scalar2=None, op0=ALU.is_lt))
            dr("vector")
            q_ve[b] = op("vector", lambda e, rows=rows, cols=cols, col=col, b=b, dst=dst, ha=ha: e.scalar_tensor_tensor(
                out=dst, in0=lbuf[b][0:rows, 0:cols], scalar=btbl[0:rows, col:col + 1],
                in1=ha, op0=ALU.is_gt, op1=ALU.subtract), sem="ve")
            lb_free[b] = q_ve[b]
            scr = [qg_scr, qu_scr, qd_scr][col]
            wait("gpsimd", "ve", q_ve[b])
            q_gd[b] = dma("gpsimd", scr[icn, 0:rows, 0:cols], qout[b][0:rows, 0:cols], f"qs{b}")


        pa = ExitStack()
        ptb = pa.enter_context(nc.sbuf_tensor("ptb", [128, 4096], BF16))
        dn1 = pa.enter_context(nc.sbuf_tensor("dn1", [1, 512], F32))
        dnr = pa.enter_context(nc.sbuf_tensor("dnr", [1, 512], F32))
        rec_b = pa.enter_context(nc.sbuf_tensor("rec_b", [128, 512], F32))
        val_n = pa.enter_context(nc.sbuf_tensor("val_n", [128, 512], BF16))
        wst = [pa.enter_context(nc.sbuf_tensor(f"wst{i}", [128, 4096], BF16)) for i in range(2)]
        ps = [pa.enter_context(nc.psum_tensor(f"ps{i}", [128, 1024], F32)) for i in range(2)]
        pav = pa.enter_context(nc.psum_tensor("pav", [128, 512], F32))
        pbc = pa.enter_context(nc.psum_tensor("pbc", [128, 512], F32))
        pwo = [pa.enter_context(nc.psum_tensor(f"pwo{i}", [128, 512], F32)) for i in range(2)]

        wait("tensor", "sc", pp_sc)
        wait("tensor", "ve", pp_ve)
        wait("tensor", "wkv", wo_ld)
        wait("scalar", "gp", gp_sum)
        wait("scalar", "ard", CNT["ard"])
        wait("sync", "sc", pp_sc)
        wait("sync", "ve", pp_ve)

        last_exp_sc = {0: 0, 1: 0}
        last_av_pe = {0: 0, 1: 0}
        wst_gd = {0: 0, 1: 0}
        vn_prev = 0
        rb_prev = 0
        wo_ev = 0
        wo_ev_p = {0: 0, 1: 0}
        pending_wo = None

        def emit_wo(qb_w, vn_w):
            nonlocal wo_ev
            wait("tensor", "ve", vn_w)
            m = qb_w // 2
            for j in range(4):
                r0 = (qb_w % 2) * 512 + j * 128
                bj = j % 2
                for h8 in range(8):
                    bw = h8 % 2
                    wait("tensor", "sc", wo_ev_p[bw])
                    op("tensor", lambda e, j=j, h8=h8, bw=bw: e.matmul(
                        pwo[bw][:, :], val_n[:, j * 128:(j + 1) * 128],
                        wo_sl[:, h8 * 512:(h8 + 1) * 512],
                        start=True, stop=True, skip_group_check=True), sem="pe")
                    wo_pe = CNT["pe"]
                    wait("scalar", "pe", wo_pe)
                    if h8 < 2:
                        wait("scalar", f"ws{bj}", wst_gd[bj])
                    wo_ev_p[bw] = op("scalar", lambda e, wb=wst[bj], h8=h8, bw=bw: e.copy(
                        out=wb[:, h8 * 512:(h8 + 1) * 512], in_=pwo[bw][:, :]), sem="sc")
                wait("gpsimd", "sc", wo_ev_p[0])
                wait("gpsimd", "sc", wo_ev_p[1])
                wst_gd[bj] = dma("gpsimd", rs_in[m, r0:r0 + 128, :], wst[bj][:, :], f"ws{bj}")
            wo_ev = max(wo_ev_p.values())

        for qb in range(NSB):
            q0 = qb * 512

            def emit_s(kc2, q0=q0):
                g = kc2 % 2
                wait("tensor", "sc", last_exp_sc[g])
                if kc2 == 0:
                    wait("tensor", "ve", vn_prev)
                    wait("tensor", "sc", wo_ev)
                    wait("tensor", "ve", rb_prev)
                for u in range(2):
                    kc = kc2 * 2 + u
                    op("tensor", lambda e, kc=kc, g=g, u=u, q0=q0: e.matmul(
                        ps[g][:, u * 512:(u + 1) * 512], kt_s[:, kc * 128:(kc + 1) * 128],
                        qt_s[:, q0:q0 + 512], start=True, stop=True, skip_group_check=True),
                       sem="pe" if u == 1 else None)
                smm = CNT["pe"]
                wait("scalar", "pe", smm)
                wait("scalar", "pe", last_av_pe[g])
                last_exp_sc[g] = op("scalar", lambda e, g=g: e.activation(
                    out=ptb[:, g * 2048:g * 2048 + 1024], in_=ps[g][:, :], func=AF.Exp), sem="sc")

            def emit_av(kc2):
                g = kc2 % 2
                wait("tensor", "sc", last_exp_sc[g])
                for u in range(2):
                    kc = kc2 * 2 + u
                    pslice = ptb[:, g * 2048 + u * 512: g * 2048 + (u + 1) * 512]
                    op("tensor", lambda e, kc=kc, pslice=pslice: e.matmul(
                        pav[:, :], v_s[:, kc * 128:(kc + 1) * 128], pslice,
                        start=(kc == 0), stop=(kc == NKC - 1), skip_group_check=True))
                    op("tensor", lambda e, kc=kc, pslice=pslice: e.matmul(
                        pbc[:, :], ones_ab[:, :], pslice,
                        start=(kc == 0), stop=(kc == NKC - 1), skip_group_check=True),
                       sem="pe" if u == 1 else None)
                last_av_pe[g] = CNT["pe"]

            emit_s(0)
            emit_s(1)
            emit_av(0)
            emit_s(2)
            emit_av(1)
            if pending_wo is not None:
                emit_wo(*pending_wo)
                pending_wo = None
            for kc2 in range(3, NKC // 2):
                emit_s(kc2)
                emit_av(kc2 - 1)
            emit_av(NKC // 2 - 1)
            av_last = CNT["pe"]

            wait("vector", "pe", av_last)
            if "dn" in tap_aps:
                dn_sc = op("scalar", lambda e: e.copy(out=dn1[0:1, :], in_=pbc[0:1, :]), sem="sc")
                wait("sync", "sc", dn_sc)
                dma("sync", tap_aps["dn"][0:1, q0:q0 + 512], dn1[0:1, :], "st")
                wait("vector", "st", CNT["st"])
            rb_prev = op("vector", lambda e: e.reciprocal(out=rec_b[:, :], in_=pbc[:, :]), sem="ve")
            rb_prev_sem = "ve"
            wait("vector", "ve", rb_prev)
            vn_prev = op("vector", lambda e: e.tensor_tensor(out=val_n[:, :], in0=pav[:, :], in1=rec_b[:, :],
                                                             op=ALU.mult), sem="ve")
            if "valn" in tap_aps:
                wait("sync", "ve", vn_prev)
                dma("sync", tap_aps["valn"][:, q0:q0 + 512], val_n[:, :], "st")
            pending_wo = (qb, vn_prev)
            for _ in range(3):
                emit_quant_tile()
        emit_wo(*pending_wo)
        while qstate["kq"] < len(qtiles):
            emit_quant_tile()
        quant_ve = max(q_ve.values())
        quant_qs = dict(q_gd)
        wo_dma_all = dict(wst_gd)
        pa_sc = CNT["sc"]
        pa_ve = CNT["ve"]
        pa.close()
        p0.close()
        qkv_scope.close()

        wdt_scope = ExitStack()
        wdt_q = wdt_scope.enter_context(nc.sbuf_tensor("wdt_q", [128, NIC * 4096], BF16))
        wait("sync", "qs0", quant_qs[0])
        wait("sync", "qs1", quant_qs[1])
        wait("sync", "sc", pa_sc)
        wait("sync", "ve", pa_ve)
        wdt_ld = 0
        for icn in range(NIC):
            w = ICW[icn]
            wdt_ld = dma("sync", wdt_q[0:w, icn * 4096:(icn + 1) * 4096], qd_scr[icn, 0:w, :], "wdl")
        pf = ExitStack()
        xn_sb = pf.enter_context(nc.sbuf_tensor("xn_sb", [128, 32 * 512], BF16))
        interT = pf.enter_context(nc.sbuf_tensor("interT", [128, NIC * 512], BF16))
        gw = [pf.enter_context(nc.sbuf_tensor(f"gw{i}", [128, 4096], BF16)) for i in range(3)]
        gts = [pf.enter_context(nc.sbuf_tensor(f"gts{i}", [128, 512], BF16)) for i in range(2)]
        fst = [pf.enter_context(nc.sbuf_tensor(f"fst{i}", [128, 1024], BF16)) for i in range(2)]
        pg = pf.enter_context(nc.psum_tensor("pg", [128, 512], F32))
        pu = pf.enter_context(nc.psum_tensor("pu", [128, 512], F32))
        pd = [pf.enter_context(nc.psum_tensor(f"pd{i}", [128, 1024], F32)) for i in range(2)]

        wait("tensor", "sc", pa_sc)
        wait("tensor", "ve", pa_ve)
        wait("sync", "sc", pa_sc)
        wait("sync", "ve", pa_ve)
        wait("sync", "ag", ag_done)
        wait("sync", "qs0", quant_qs[0])
        wait("sync", "qs1", quant_qs[1])
        wait("gpsimd", "ws0", wo_dma_all[0])
        wait("gpsimd", "ws1", wo_dma_all[1])
        wait("sync", "ws0", wo_dma_all[0])
        wait("sync", "ws1", wo_dma_all[1])

        wait("scalar", "ard", btbl_ard)

        def xn_src1(sb):
            r, cs = sb // 2, (sb % 2) * 512
            return ag_o1[r * 2048:(r + 1) * 2048, cs:cs + 512].rearrange("(hc p) q -> p hc q", p=128)

        def xn_src2(sb):
            r, cs = sb // 2, (sb % 2) * 512
            return ag_o2[r * 2048:(r + 1) * 2048, cs:cs + 512].rearrange("(hc p) q -> p hc q", p=128)

        xn_dst1 = xn_sb.ap()[:, 0:16 * 512].rearrange("p (hc q) -> p hc q", q=512)
        xn_dst2 = xn_sb.ap()[:, 16 * 512:32 * 512].rearrange("p (hc q) -> p hc q", q=512)

        def gbuf(sb, icn, which):
            return (2 * (sb * NIC + icn) + which) % 3

        GWS = ["gwa", "gwb", "gwc"]
        dma("sync", xn_dst1, xn_src1(0), "xns")
        xn_ld = {0: dma("sync", xn_dst2, xn_src2(0), "xns")}
        gw_ld = {}
        gw_pe = {0: 0, 1: 0, 2: 0}
        gw_ld[(0, 0, 0)] = dma("sync", gw[0][:, 0:4096], qg_scr[0, :, :], "gwa")
        gw_ld[(0, 0, 1)] = dma("sync", gw[1][:, 0:4096], qu_scr[0, :, :], "gwb")

        g_ev = 0
        i_ve = 0
        down_pe_done = 0
        d_ev = {0: 0, 1: 0}
        f_dma = {0: 0, 1: 0}
        gts_free = {0: 0, 1: 0}
        rs_launched = {}

        for sb in range(NSB):
            m, r0 = sb // 2, (sb % 2) * 512
            for icn in range(NIC):
                w = ICW[icn]
                gb, ub = gbuf(sb, icn, 0), gbuf(sb, icn, 1)
                wait("tensor", GWS[gb], gw_ld[(sb, icn, 0)])
                if icn == 0:
                    wait("tensor", "xns", xn_ld[sb])
                wait("tensor", "sc", g_ev)
                for hc in range(32):
                    op("tensor", lambda e, hc=hc, w=w, gb=gb: e.matmul(
                        pg[0:w, :], gw[gb][:, hc * w:(hc + 1) * w], xn_sb[:, hc * 512:(hc + 1) * 512],
                        start=(hc == 0), stop=(hc == 31)), sem="pe" if hc == 31 else None)
                g_mm = CNT["pe"]
                gw_pe[gb] = g_mm
                wait("tensor", GWS[ub], gw_ld[(sb, icn, 1)])
                wait("tensor", "ve", i_ve)
                for hc in range(32):
                    op("tensor", lambda e, hc=hc, w=w, ub=ub: e.matmul(
                        pu[0:w, :], gw[ub][:, hc * w:(hc + 1) * w], xn_sb[:, hc * 512:(hc + 1) * 512],
                        start=(hc == 0), stop=(hc == 31)), sem="pe" if hc == 31 else None)
                u_mm = CNT["pe"]
                gw_pe[ub] = u_mm
                nsb, nic = (sb, icn + 1) if icn + 1 < NIC else (sb + 1, 0)
                if nsb < NSB:
                    for which in range(2):
                        nb = gbuf(nsb, nic, which)
                        wait("sync", "pe", gw_pe[nb])
                        scr = qg_scr if which == 0 else qu_scr
                        cols = 32 * ICW[nic]
                        gw_ld[(nsb, nic, which)] = dma("sync", gw[nb][:, 0:cols], scr[nic, 0:128, 0:cols], GWS[nb])
                bg = icn % 2
                wait("scalar", "pe", g_mm)
                wait("scalar", "ve", gts_free[bg])
                g_ev = op("scalar", lambda e, w=w, b2=gts[bg]: e.copy(out=b2[0:w, :], in_=pg[0:w, :]), sem="sc")
                wait("vector", "sc", g_ev)
                wait("vector", "pe", u_mm)
                if sb > 0 and icn == 0:
                    wait("vector", "pe", down_pe_done)
                i_ve = op("vector", lambda e, w=w, icn=icn, b2=gts[bg]: e.tensor_tensor(
                    out=interT[0:w, icn * 512:(icn + 1) * 512], in0=b2[0:w, :], in1=pu[0:w, :],
                    op=ALU.mult), sem="ve")
                gts_free[bg] = i_ve
            gu_done_pe = CNT["pe"]
            inter_done_ve = CNT["ve"]
            if sb + 1 < NSB:
                wait("sync", "pe", gu_done_pe)
                dma("sync", xn_dst1, xn_src1(sb + 1), "xns")
                xn_ld[sb + 1] = dma("sync", xn_dst2, xn_src2(sb + 1), "xns")
            if sb == 0:
                wait("tensor", "wdl", wdt_ld)
            wait("tensor", "ve", inter_done_ve)
            kk = 0
            for qt in range(4):
                for quar in range(4):
                    pb = pd[kk % 2]
                    wait("tensor", "sc", d_ev[kk % 2])
                    for icn in range(NIC):
                        w = ICW[icn]
                        lhs = interT[0:w, icn * 512 + qt * 128: icn * 512 + (qt + 1) * 128]
                        for u in range(2):
                            op("tensor", lambda e, w=w, icn=icn, lhs=lhs, quar=quar, u=u, pb=pb: e.matmul(
                                pb[:, u * 512:(u + 1) * 512], lhs,
                                wdt_q[0:w, icn * 4096 + quar * 1024 + u * 512: icn * 4096 + quar * 1024 + (u + 1) * 512],
                                start=(icn == 0), stop=(icn == NIC - 1)),
                               sem="pe" if (u == 1 and icn == NIC - 1) else None)
                    d_mm = CNT["pe"]
                    fb = fst[kk % 2]
                    wait("scalar", "pe", d_mm)
                    wait("scalar", f"fd{kk % 2}", f_dma[kk % 2])
                    d_ev[kk % 2] = op("scalar", lambda e, fb=fb, pb=pb: e.activation(
                        out=fb[:, :], in_=pb[:, :], func=AF.Copy, scale=btbl[:, 6:7]), sem="sc")
                    wait("gpsimd", "sc", d_ev[kk % 2])
                    f_dma[kk % 2] = dma("gpsimd",
                                        rs_in[m, r0 + qt * 128: r0 + (qt + 1) * 128, quar * 1024:(quar + 1) * 1024],
                                        fb[:, :], f"fd{kk % 2}", accum_op=ALU.add)
                    kk += 1
            down_pe_done = CNT["pe"]
            if sb % 2 == 1:
                wait("gpsimd", "fd0", CNT["fd0"])
                wait("gpsimd", "fd1", CNT["fd1"])
                rs_launched[m] = coll("ReduceScatter", ALU.add,
                                      [rs_in[m].opt()], [rs_o[m].opt()], f"rs{m}")
        if "interT" in tap_aps:
            wait("sync", "ve", CNT["ve"])
            dma("sync", tap_aps["interT"], interT[:, :], "st")
        if "rsin0" in tap_aps:
            wait("sync", "fd0", CNT["fd0"])
            wait("sync", "fd1", CNT["fd1"])
            dma("sync", tap_aps["rsin0"], rs_in[0], "st")
        pf_sc = CNT["sc"]
        pf_pe = CNT["pe"]
        pf.close()
        wdt_scope.close()

        fin = ExitStack()
        fina = [fin.enter_context(nc.sbuf_tensor(f"fina{i}", [128, 4096], BF16)) for i in range(2)]
        finb = [fin.enter_context(nc.sbuf_tensor(f"finb{i}", [128, 4096], F32)) for i in range(2)]
        fino = [fin.enter_context(nc.sbuf_tensor(f"fino{i}", [128, 4096], F32)) for i in range(2)]
        wait("sync", "sc", pf_sc)
        wait("sync", "pe", pf_pe)
        wait("vector", "fd0", CNT["fd0"])
        wait("vector", "fd1", CNT["fd1"])
        add_ve = {0: 0, 1: 0}
        out_st = {0: 0, 1: 0}
        for m in range(8):
            b = m % 2
            wait("sync", f"rs{m}", rs_launched[m])
            wait("sync", "ve", add_ve[b])
            if out_st[b]:
                wait("sync", "st", CNT["st"])
            la = dma("sync", fina[b][:, :], rs_o[m], f"ff{b}")
            lb2 = dma("sync", finb[b][:, :], x_res[m * 128:(m + 1) * 128, :], f"ff{b}")
            wait("vector", f"ff{b}", lb2)
            add_ve[b] = op("vector", lambda e, b=b: e.tensor_tensor(
                out=fino[b][:, :], in0=fina[b][:, :], in1=finb[b][:, :], op=ALU.add), sem="ve")
            wait("gpsimd", "ve", add_ve[b])
            out_st[b] = dma("gpsimd", out_c[m * 128:(m + 1) * 128, :], fino[b][:, :], "st")
        wait("sync", "st", CNT["st"])
        fin.close()

        block = ec(nc.Block())

        @block.sync
        def _(eng):
            for e, fn in PROG:
                if e == "sync":
                    fn(eng)

        @block.gpsimd
        def _(eng):
            for e, fn in PROG:
                if e == "gpsimd":
                    fn(eng)

        @block.vector
        def _(eng):
            for e, fn in PROG:
                if e == "vector":
                    fn(eng)

        @block.scalar
        def _(eng):
            for e, fn in PROG:
                if e == "scalar":
                    fn(eng)

        @block.tensor
        def _(eng):
            for e, fn in PROG:
                if e == "tensor":
                    fn(eng)

    return nc


def make_in_maps(L0_attn_x, L0_attn_wq, L0_attn_wk, L0_attn_wv, L0_attn_wo,
                 L0_ffn_w_gate, L0_ffn_w_up, L0_ffn_w_down):
    x2 = np.ascontiguousarray(np.asarray(L0_attn_x, dtype=np.float32).reshape(S, H))
    wq = np.asarray(L0_attn_wq, dtype=np.float32)
    wk = np.asarray(L0_attn_wk, dtype=np.float32)
    wv = np.asarray(L0_attn_wv, dtype=np.float32)
    wo = np.asarray(L0_attn_wo, dtype=np.float32)
    wgT_f = np.ascontiguousarray(np.asarray(L0_ffn_w_gate, dtype=np.float32).T)
    wuT_f = np.ascontiguousarray(np.asarray(L0_ffn_w_up, dtype=np.float32).T)
    wdT_f = np.ascontiguousarray(np.asarray(L0_ffn_w_down, dtype=np.float32).T)
    in_maps = []
    for c in range(NCORE):
        xr = np.concatenate([x2[m * 1024 + c * 128: m * 1024 + (c + 1) * 128] for m in range(8)], axis=0)
        in_maps.append({
            "x_ln": np.ascontiguousarray(x2[c * SLOC:(c + 1) * SLOC]),
            "x_res": np.ascontiguousarray(xr),
            "wq4": np.ascontiguousarray(wq[:, c * 512:(c + 1) * 512]),
            "wk1": np.ascontiguousarray(wk[:, c * 128:(c + 1) * 128]),
            "wv1": np.ascontiguousarray(wv[:, c * 128:(c + 1) * 128]),
            "wo1": np.ascontiguousarray(wo[c * 128:(c + 1) * 128, :]),
            "wgT": np.ascontiguousarray(wgT_f[:, c * IPC:(c + 1) * IPC]),
            "wuT": np.ascontiguousarray(wuT_f[:, c * IPC:(c + 1) * IPC]),
            "wdT": np.ascontiguousarray(wdT_f[c * IPC:(c + 1) * IPC, :]),
        })
    return in_maps


def assemble(results):
    out = np.empty((S, H), dtype=np.float32)
    for c in range(NCORE):
        oc = results[c]["out_c"]
        for m in range(8):
            out[m * 1024 + c * 128: m * 1024 + (c + 1) * 128] = oc[m * 128:(m + 1) * 128]
    return out.reshape(1, S, H)


_NC_CACHE = {}


def _get_nc(taps=()):
    key = tuple(sorted(taps))
    if key not in _NC_CACHE:
        _NC_CACHE[key] = build(taps=key)
    return _NC_CACHE[key]


def run(inputs, taps=()):
    nc = _get_nc(taps)
    in_maps = make_in_maps(**inputs)
    return run_bass_kernel_spmd(nc, in_maps, core_ids=list(range(NCORE)))


def kernel(**inputs):
    res = run(inputs)
    return assemble(res.results)


# version 15 (final)
# speedup vs baseline: 1.0520x; 1.0520x over previous
import numpy as np
import ml_dtypes
from contextlib import ExitStack

import concourse.bass as bass
import concourse.mybir as mybir
from concourse.bass_utils import run_bass_kernel_spmd

F32 = mybir.dt.float32
BF16 = mybir.dt.bfloat16
AF = mybir.ActivationFunctionType
ALU = mybir.AluOpType
AX = mybir.AxisListType

NCORE = 8
H = 4096
S = 8192
D = 128
SLOC = S // NCORE
IPC = 11008 // NCORE
NIC = 11
ICW = [128] * 10 + [96]
ICOFF = [128 * i for i in range(11)]
NSB = 16
NKC = 64
EPS = 1e-5
NW_FULL = 11008 * 4096
RSQ = 0.08838834764831845
RG = [list(range(NCORE))]


def build(taps=()):
    nc = bass.Bass("TRN2", target_bir_lowering=False)

    x_ln = nc.dram_tensor("x_ln", [SLOC, H], F32, kind="ExternalInput").ap()
    x_res = nc.dram_tensor("x_res", [SLOC, H], F32, kind="ExternalInput").ap()
    wq4 = nc.dram_tensor("wq4", [H, 512], F32, kind="ExternalInput").ap()
    wk1 = nc.dram_tensor("wk1", [H, D], F32, kind="ExternalInput").ap()
    wv1 = nc.dram_tensor("wv1", [H, D], F32, kind="ExternalInput").ap()
    wo1 = nc.dram_tensor("wo1", [D, H], F32, kind="ExternalInput").ap()
    wgT = nc.dram_tensor("wgT", [H, IPC], F32, kind="ExternalInput").ap()
    wuT = nc.dram_tensor("wuT", [H, IPC], F32, kind="ExternalInput").ap()
    wdT = nc.dram_tensor("wdT", [IPC, H], F32, kind="ExternalInput").ap()
    out_c = nc.dram_tensor("out_c", [SLOC, H], F32, kind="ExternalOutput").ap()

    ag_in = nc.dram_tensor("ag_in", [H, SLOC], BF16).ap()
    ag_o1 = nc.dram_tensor("ag_o1", [NCORE * 2048, SLOC], BF16, addr_space="Shared").ap()
    ag_o2 = nc.dram_tensor("ag_o2", [NCORE * 2048, SLOC], BF16, addr_space="Shared").ap()
    ar_in = nc.dram_tensor("ar_in", [128, 3], F32).ap()
    ar_o = nc.dram_tensor("ar_o", [128, 3], F32).ap()
    btb_d = nc.dram_tensor("btb_d", [1, 8], F32).ap()
    qg_scr = nc.dram_tensor("qg_scr", [NIC, 128, 4096], BF16).ap()
    qu_scr = nc.dram_tensor("qu_scr", [NIC, 128, 4096], BF16).ap()
    qd_scr = nc.dram_tensor("qd_scr", [NIC, 128, 4096], BF16).ap()
    rs_in = nc.dram_tensor("rs_in", [8, 1024, H], BF16).ap()
    rs_o = nc.dram_tensor("rs_o", [8, 128, H], BF16).ap()

    tap_aps = {}
    tap_shapes = {
        "qt": ([D, S], BF16), "kt": ([D, S], BF16), "vs": ([128, S], BF16),
        "dn": ([1, S], F32), "valn": ([D, S], BF16),
        "interT": ([128, NIC * 512], BF16), "rsin0": ([1024, H], F32),
        "wdtq": ([128, NIC * 4096], BF16), "btbl": ([128, 8], F32),
        "agin": ([H, SLOC], BF16),
    }
    for t in taps:
        sh, dt = tap_shapes[t]
        tap_aps[t] = nc.dram_tensor("tap_" + t, sh, dt, kind="ExternalOutput").ap()

    PROG = []
    CNT = {}
    SEMS = {}

    def em(eng, fn):
        PROG.append((eng, fn))

    def inc(sem, n):
        CNT[sem] = CNT.get(sem, 0) + n
        return CNT[sem]

    def dma(q, out, in_, sem, **kw):
        t = inc(sem, 16)
        em(q, lambda e, o=out, i=in_, kw=dict(kw), s=sem: e.dma_start(out=o, in_=i, **kw).then_inc(SEMS[s], 16))
        return t

    def op(eng, fn, sem=None):
        if sem is None:
            em(eng, fn)
            return None
        t = inc(sem, 1)
        em(eng, lambda e, f=fn, s=sem: f(e).then_inc(SEMS[s], 1))
        return t

    DMA_SEMS = {"lb0", "lb1", "qlb0", "qlb1", "agw0", "agw1", "ard", "qs0", "qs1", "wkv", "wdl", "xq0",
                "xq1", "gwa", "gwb", "gwc", "xns", "ws0", "ws1", "fd0", "fd1",
                "ff0", "ff1", "st"}

    def wait(eng, sem, val):
        if not val:
            return
        assert val <= CNT.get(sem, 0), f"wait on future count {sem}={val} > {CNT.get(sem, 0)}"
        if sem in DMA_SEMS:
            assert val == CNT[sem], f"non-prefix DMA wait {sem}: {val} != {CNT[sem]}"
        em(eng, lambda e, s=sem, v=val: e.wait_ge(SEMS[s], v))

    def dr(eng):
        em(eng, lambda e: e.drain())

    def coll(kind, alu, ins, outs, sem):
        t = inc(sem, 1)
        em("gpsimd", lambda e, k=kind, a=alu, i=ins, o=outs, s=sem:
           e.collective_compute(k, a, replica_groups=RG, ins=i, outs=o).then_inc(SEMS[s], 1))
        return t

    with ExitStack() as top:
        ec = top.enter_context
        for sn in (["ld", "gd", "pe", "sc", "ve", "gp", "st", "ag", "ar",
                    "lb0", "lb1", "agw0", "agw1", "ard", "qs0", "qs1", "wkv",
                    "xq0", "xq1", "gwa", "gwb", "gwc", "xns", "wdl", "ws0", "ws1", "qlb0", "qlb1",
                    "fd0", "fd1", "ff0", "ff1"] + [f"rs{m}" for m in range(8)]):
            SEMS[sn] = ec(nc.semaphore(sn))

        ident = ec(nc.sbuf_tensor("ident", [128, 128], BF16))
        ones_a = ec(nc.sbuf_tensor("ones_a", [128, 1], F32))
        ones_ab = ec(nc.sbuf_tensor("ones_ab", [128, 128], BF16))
        ones_b = ec(nc.sbuf_tensor("ones_b", [1, 128], F32))
        btbl = ec(nc.sbuf_tensor("btbl", [128, 8], F32))
        sc_t = ec(nc.sbuf_tensor("sc_t", [1, 16], F32))
        wo_sl = ec(nc.sbuf_tensor("wo_sl", [128, H], BF16))

        ptr_scope = ExitStack()
        ptr = [ptr_scope.enter_context(nc.psum_tensor(f"ptr{i}", [128, 1024], BF16)) for i in range(2)]
        qkv_scope = ExitStack()
        qt_s = qkv_scope.enter_context(nc.sbuf_tensor("qt_s", [128, S], BF16, side="right"))
        kt_s = qkv_scope.enter_context(nc.sbuf_tensor("kt_s", [128, S], BF16, side="right"))
        v_s = qkv_scope.enter_context(nc.sbuf_tensor("v_s", [128, S], BF16, side="right"))
        vst = qkv_scope.enter_context(nc.sbuf_tensor("vst", [128, 512], BF16, side="right"))
        stat_scope = ExitStack()
        wq_s = stat_scope.enter_context(nc.sbuf_tensor("wq_s", [128, 4096], BF16, side="right"))
        xnq = [stat_scope.enter_context(nc.sbuf_tensor(f"xnq{i}", [128, 2048], BF16, side="right")) for i in range(2)]
        wk_s = stat_scope.enter_context(nc.sbuf_tensor("wk_s", [128, 4096], BF16, side="right"))
        wv_s = stat_scope.enter_context(nc.sbuf_tensor("wv_s", [128, 4096], BF16, side="right"))

        p0 = ExitStack()
        lbuf = [p0.enter_context(nc.sbuf_tensor(f"lbuf{i}", [128, 4096], F32)) for i in range(2)]
        xnb = [p0.enter_context(nc.sbuf_tensor(f"xnb{i}", [128, 4096], BF16)) for i in range(2)]
        snk = p0.enter_context(nc.sbuf_tensor("snk", [128, 4096], BF16))
        agst = [p0.enter_context(nc.sbuf_tensor("agst0", [128, 4096], BF16))] * 2
        st_m = p0.enter_context(nc.sbuf_tensor("st_m", [128, 6], F32))

        wk_ld = dma("gpsimd", wk_s.ap().rearrange("p (hc d) -> p hc d", d=128),
                    wk1.rearrange("(hc p) d -> p hc d", p=128), "wkv")
        wv_ld = dma("gpsimd", wv_s.ap().rearrange("p (hc d) -> p hc d", d=128),
                    wv1.rearrange("(hc p) d -> p hc d", p=128), "wkv")
        wo_ld = dma("gpsimd", wo_sl[:, :], wo1[:, :], "wkv")
        op("gpsimd", lambda e: e.memset(ident[:, :], 0.0))
        dr("gpsimd")
        ident_gp = op("gpsimd", lambda e: e.affine_select(
            out=ident[:, :], in_=ident[:, :], compare_op=ALU.not_equal,
            fill=1.0, base=0, pattern=[[-1, 128]], channel_multiplier=1), sem="gp")
        op("vector", lambda e: e.memset(ones_a[:, :], 1.0))
        op("vector", lambda e: e.memset(ones_ab[:, :], 1.0))
        op("vector", lambda e: e.memset(ones_b[:, :], 1.0), sem="ve")

        ln_done_ve = [0] * 8
        tr_done_pe = [0] * 8
        ag_dma_done = [0] * 8
        cp_sc = {}

        for t in range(8):
            b = t % 2
            if t >= 2:
                wait("sync", "ve", ln_done_ve[t - 2])
            ld_t = dma("sync", lbuf[b][:, :], x_ln[t * 128:(t + 1) * 128, :], f"lb{b}")
            wait("vector", f"lb{b}", ld_t)
            op("vector", lambda e, b=b: e.reduce_sum(out=st_m[:, 0:1], in_=lbuf[b][:, :], axis=AX.X), sem="ve")
            dr("vector")
            wait("scalar", f"lb{b}", ld_t)
            if t >= 1:
                wait("scalar", "ve", ln_done_ve[t - 1])
            sq_t = op("scalar", lambda e, b=b: e.activation(
                out=snk[:, :], in_=lbuf[b][:, :], func=AF.Square,
                accum_out=st_m[:, 1:2]), sem="sc")
            wait("vector", "sc", sq_t)
            op("vector", lambda e: e.tensor_scalar(out=st_m[:, 2:3], in0=st_m[:, 0:1],
                                                   scalar1=1.0 / H, scalar2=None, op0=ALU.mult))
            dr("vector")
            op("vector", lambda e: e.tensor_tensor(out=st_m[:, 3:4], in0=st_m[:, 2:3], in1=st_m[:, 2:3], op=ALU.mult))
            op("vector", lambda e: e.tensor_scalar(out=st_m[:, 4:5], in0=st_m[:, 1:2],
                                                   scalar1=1.0 / H, scalar2=None, op0=ALU.mult))
            dr("vector")
            op("vector", lambda e: e.tensor_tensor(out=st_m[:, 4:5], in0=st_m[:, 4:5], in1=st_m[:, 3:4], op=ALU.subtract))
            dr("vector")
            op("vector", lambda e: e.tensor_scalar(out=st_m[:, 4:5], in0=st_m[:, 4:5],
                                                   scalar1=EPS, scalar2=None, op0=ALU.add))
            dr("vector")
            rc_t = op("vector", lambda e: e.reciprocal(out=st_m[:, 5:6], in_=st_m[:, 4:5]), sem="ve")
            wait("scalar", "ve", rc_t)
            rs_t = op("scalar", lambda e: e.activation(out=st_m[:, 5:6], in_=st_m[:, 5:6], func=AF.Sqrt), sem="sc")
            wait("vector", "sc", rs_t)
            op("vector", lambda e: e.tensor_tensor(out=st_m[:, 3:4], in0=st_m[:, 2:3], in1=st_m[:, 5:6], op=ALU.mult))
            dr("vector")
            op("vector", lambda e: e.tensor_scalar(out=st_m[:, 3:4], in0=st_m[:, 3:4],
                                                   scalar1=-1.0, scalar2=None, op0=ALU.mult))
            dr("vector")
            if t >= 2:
                wait("vector", "pe", tr_done_pe[t - 2])
            op("vector", lambda e, b=b: e.tensor_scalar(
                out=xnb[b][:, :], in0=lbuf[b][:, :],
                scalar1=st_m[:, 5:6], scalar2=st_m[:, 3:4], op0=ALU.mult, op1=ALU.add), sem="ve")
            ln_done_ve[t] = CNT["ve"]

            wait("tensor", "ve", ln_done_ve[t])
            if t == 0:
                wait("tensor", "gp", ident_gp)
            if t >= 1:
                wait("scalar", f"agw{(t - 1) % 2}", ag_dma_done[t - 1])
            for g in range(8):
                pp2 = ptr[g % 2]
                prev = cp_sc.get((t, g - 2), cp_sc.get((t - 1, g + 6)))
                wait("tensor", "sc", prev)
                for j in range(4):
                    hc = g * 4 + j
                    op("tensor", lambda e, b=b, hc=hc, pp2=pp2, j=j: e.transpose(
                        pp2[:, j * 128:(j + 1) * 128], xnb[b][:, hc * 128:(hc + 1) * 128], ident[:, :]),
                       sem="pe" if j == 3 else None)
                tg = CNT["pe"]
                wait("scalar", "pe", tg)
                cp_sc[(t, g)] = op("scalar", lambda e, b=b, g=g, pp2=pp2: e.copy(
                    out=agst[b][:, g * 512:(g + 1) * 512], in_=pp2[:, 0:512]), sem="sc")
            tr_done_pe[t] = CNT["pe"]
            wait("gpsimd", "sc", cp_sc[(t, 7)])
            ag_view = ag_in[:, t * 128:(t + 1) * 128].rearrange("(hc p) j -> p hc j", p=128)
            agr = agst[b].rearrange("p (hc j) -> p hc j", j=128)
            ag_dma_done[t] = dma("gpsimd", ag_view, agr, f"agw{t % 2}")

        wait("gpsimd", "agw0", ag_dma_done[6])
        wait("gpsimd", "agw1", ag_dma_done[7])
        ag_h1 = coll("AllGather", ALU.bypass, [ag_in[0:2048, :].opt()], [ag_o1[:, :].opt()], "ag")
        ag_done = coll("AllGather", ALU.bypass, [ag_in[2048:4096, :].opt()], [ag_o2[:, :].opt()], "ag")
        if "agin" in tap_aps:
            dma("gpsimd", tap_aps["agin"], ag_in[:, :], "st")

        lb_free = {0: ln_done_ve[6], 1: ln_done_ve[7]}
        for hc in range(32):
            b = hc % 2
            wait("sync", "ve", lb_free[b])
            ldt = dma("sync", lbuf[b][:, 0:512], wq4[hc * 128:(hc + 1) * 128, :], f"lb{b}")
            wait("vector", f"lb{b}", ldt)
            bb = lbuf[b]
            op("vector", lambda e, bb=bb: e.tensor_tensor(out=bb[:, 0:128], in0=bb[:, 0:128], in1=bb[:, 128:256], op=ALU.add))
            op("vector", lambda e, bb=bb: e.tensor_tensor(out=bb[:, 256:384], in0=bb[:, 256:384], in1=bb[:, 384:512], op=ALU.add))
            dr("vector")
            op("vector", lambda e, bb=bb: e.tensor_tensor(out=bb[:, 0:128], in0=bb[:, 0:128], in1=bb[:, 256:384], op=ALU.add))
            dr("vector")
            lb_free[b] = op("vector", lambda e, bb=bb, hc=hc: e.tensor_scalar(
                out=wq_s[:, hc * 128:(hc + 1) * 128], in0=bb[:, 0:128],
                scalar1=RSQ, scalar2=None, op0=ALU.mult), sem="ve")
        wq_done = CNT["ve"]

        acc = p0.enter_context(nc.sbuf_tensor("acc", [128, 4], F32))
        tmp1 = p0.enter_context(nc.sbuf_tensor("tmp1", [128, 2], F32))
        tmpb = p0.enter_context(nc.sbuf_tensor("tmpb", [128, 4096], BF16))
        qout = [p0.enter_context(nc.sbuf_tensor(f"qout{i}", [128, 4096], BF16)) for i in range(2)]

        op("vector", lambda e: e.memset(acc[:, :], 0.0), sem="ve")
        srcs = [(wgT, 0), (wuT, 1), (wdT, 2)]
        k = 0
        for (src, col) in srcs:
            if col < 2:
                tiles = [(src[n * 128:(n + 1) * 128, :], 128, IPC) for n in range(32)]
            else:
                tiles = [(src[ICOFF[n]:ICOFF[n] + ICW[n], :], ICW[n], 4096) for n in range(NIC)]
            for (tap, rows, cols) in tiles:
                b = k % 2
                qn, sname = ("sync", "lb0") if b == 0 else ("gpsimd", "qlb1")
                wait(qn, "ve", lb_free[b])
                ldt = dma(qn, lbuf[b][0:rows, 0:cols], tap, sname)
                wait("vector", sname, ldt)
                op("vector", lambda e, rows=rows, cols=cols, b=b: e.reduce_sum(
                    out=tmp1[0:rows, 0:1], in_=lbuf[b][0:rows, 0:cols], axis=AX.X,
                    apply_absolute_value=True))
                dr("vector")
                lb_free[b] = op("vector", lambda e, rows=rows, col=col: e.tensor_tensor(
                    out=acc[0:rows, col:col + 1], in0=acc[0:rows, col:col + 1],
                    in1=tmp1[0:rows, 0:1], op=ALU.add), sem="ve")
                dr("vector")
                k += 1

        wait("gpsimd", "ve", CNT["ve"])
        ar_w = dma("gpsimd", ar_in[:, :], acc[:, 0:3], "ard")
        wait("gpsimd", "ard", ar_w)
        ar_done = coll("AllReduce", ALU.add, [ar_in[:, :].opt()], [ar_o[:, :].opt()], "ar")
        wait("gpsimd", "ar", ar_done)
        ar_r = dma("gpsimd", acc[:, 0:3], ar_o[:, :], "ard")

        wait("gpsimd", "ard", ar_r)
        gp_sum = op("gpsimd", lambda e: e.tensor_reduce(
            out=sc_t[0:1, 0:3], in_=acc[:, 0:3], axis=AX.C, op=ALU.add), sem="gp")
        wait("vector", "gp", gp_sum)
        dr("vector")
        op("vector", lambda e: e.tensor_scalar(out=sc_t[0:1, 4:7], in0=sc_t[0:1, 0:3],
                                               scalar1=0.5 / NW_FULL, scalar2=EPS / 2, op0=ALU.mult, op1=ALU.add))
        dr("vector")
        op("vector", lambda e: e.tensor_scalar(out=sc_t[0:1, 8:11], in0=sc_t[0:1, 4:7],
                                               scalar1=-1.0, scalar2=None, op0=ALU.mult))
        dr("vector")
        op("vector", lambda e: e.tensor_tensor(out=sc_t[0:1, 12:13], in0=sc_t[0:1, 0:1], in1=sc_t[0:1, 1:2], op=ALU.mult))
        dr("vector")
        op("vector", lambda e: e.tensor_tensor(out=sc_t[0:1, 12:13], in0=sc_t[0:1, 12:13], in1=sc_t[0:1, 2:3], op=ALU.mult))
        dr("vector")
        op("vector", lambda e: e.tensor_scalar(out=sc_t[0:1, 13:14], in0=sc_t[0:1, 12:13],
                                               scalar1=(1.0 / NW_FULL) ** 3, scalar2=None, op0=ALU.mult))
        dr("vector")
        op("vector", lambda e: e.tensor_copy(out=sc_t[0:1, 0:3], in_=sc_t[0:1, 4:7]))
        dr("vector")
        op("vector", lambda e: e.tensor_copy(out=sc_t[0:1, 3:6], in_=sc_t[0:1, 8:11]))
        dr("vector")
        op("vector", lambda e: e.tensor_copy(out=sc_t[0:1, 6:7], in_=sc_t[0:1, 13:14]))
        dr("vector")
        tbl_ve = op("vector", lambda e: e.memset(sc_t[0:1, 7:8], 0.0), sem="ve")
        wait("gpsimd", "ve", tbl_ve)
        bw_t = dma("gpsimd", btb_d[:, :], sc_t[0:1, 0:8], "ard")
        wait("gpsimd", "ard", bw_t)
        btbl_ard = dma("gpsimd", btbl[:, :], btb_d[0:1, :].partition_broadcast(128), "ard")
        if "btbl" in tap_aps:
            wait("sync", "ard", btbl_ard)
            dma("sync", tap_aps["btbl"], btbl[:, :], "st")


        pp = ExitStack()
        pq = pp.enter_context(nc.psum_tensor("pq", [128, 512], F32))
        pk = pp.enter_context(nc.psum_tensor("pk", [128, 512], F32))
        pv = pp.enter_context(nc.psum_tensor("pv", [128, 512], F32))

        wait("scalar", "ag", ag_h1)

        wait("tensor", "ve", wq_done)
        wait("tensor", "wkv", wo_ld)
        xnq_pe = {0: 0, 1: 0}
        vtr_ve = {0: 0, 1: 0}
        evac_sc = 0
        vst_pe = 0
        for sb in range(NSB):
            r, cs = sb // 2, (sb % 2) * 512
            for hg in range(8):
                bx = hg % 2
                if hg < 4:
                    src = ag_o1[r * 2048 + hg * 512: r * 2048 + (hg + 1) * 512, cs:cs + 512]
                else:
                    src = ag_o2[r * 2048 + (hg - 4) * 512: r * 2048 + (hg - 3) * 512, cs:cs + 512]
                src = src.rearrange("(hc p) q -> p hc q", p=128)
                if sb == 0 and hg == 4:
                    wait("scalar", "ag", ag_done)
                wait("scalar", "pe", xnq_pe[bx])
                ldt = dma("scalar", xnq[bx].ap().rearrange("p (hc q) -> p hc q", q=512), src, f"xq{bx}")
                wait("tensor", f"xq{bx}", ldt)
                if hg == 0 and sb > 0:
                    wait("tensor", "sc", evac_sc)
                for j in range(4):
                    hc = hg * 4 + j
                    st0, sp0 = (hc == 0), (hc == 31)
                    rhs = xnq[bx][:, j * 512:(j + 1) * 512]
                    op("tensor", lambda e, hc=hc, rhs=rhs, st0=st0, sp0=sp0: e.matmul(
                        pq[:, :], wq_s[:, hc * 128:(hc + 1) * 128], rhs, start=st0, stop=sp0))
                    op("tensor", lambda e, hc=hc, rhs=rhs, st0=st0, sp0=sp0: e.matmul(
                        pk[:, :], wk_s[:, hc * 128:(hc + 1) * 128], rhs, start=st0, stop=sp0))
                    op("tensor", lambda e, hc=hc, rhs=rhs, st0=st0, sp0=sp0: e.matmul(
                        pv[:, :], wv_s[:, hc * 128:(hc + 1) * 128], rhs, start=st0, stop=sp0),
                       sem="pe" if j == 3 else None)
                xnq_pe[bx] = CNT["pe"]
            mm_last = CNT["pe"]
            wait("scalar", "pe", mm_last)
            op("scalar", lambda e, sb=sb: e.copy(out=qt_s[:, sb * 512:(sb + 1) * 512], in_=pq[:, :]))
            op("scalar", lambda e, sb=sb: e.copy(out=kt_s[:, sb * 512:(sb + 1) * 512], in_=pk[:, :]))
            wait("scalar", "pe", vst_pe)
            evac_sc = op("scalar", lambda e: e.copy(out=vst[:, :], in_=pv[:, :]), sem="sc")
            wait("tensor", "sc", evac_sc)
            for j in range(4):
                bp = j % 2
                wait("tensor", "sc", vtr_ve[bp])
                tj = op("tensor", lambda e, j=j, pp2=ptr[bp]: e.transpose(
                    pp2[:, 0:128], vst[:, j * 128:(j + 1) * 128], ident[:, :]), sem="pe")
                wait("scalar", "pe", tj)
                vtr_ve[bp] = op("scalar", lambda e, sb=sb, j=j, pp2=ptr[bp]: e.copy(
                    out=v_s[:, (sb * 4 + j) * 128:(sb * 4 + j + 1) * 128], in_=pp2[:, 0:128]), sem="sc")
            vst_pe = CNT["pe"]

        pp_sc = CNT["sc"]
        pp_ve = CNT["ve"]
        pp_pe = CNT["pe"]
        if "qt" in tap_aps:
            wait("sync", "sc", pp_sc)
            dma("sync", tap_aps["qt"], qt_s[:, :], "st")
        if "kt" in tap_aps:
            wait("sync", "sc", pp_sc)
            dma("sync", tap_aps["kt"], kt_s[:, :], "st")
        if "vs" in tap_aps:
            wait("sync", "ve", pp_ve)
            dma("sync", tap_aps["vs"], v_s[:, :], "st")
        pp.close()
        stat_scope.close()
        ptr_scope.close()

        tmpa = [xnb[0], xnb[1]]
        q_gd = {0: 0, 1: 0}
        q_ve = {0: 0, 1: 0}
        qtiles = []
        for (src_, col) in srcs:
            for icn in range(NIC):
                qtiles.append((src_, col, icn))
        qstate = {"kq": 0}

        def emit_quant_tile():
            kq = qstate["kq"]
            if kq >= len(qtiles):
                return
            qstate["kq"] = kq + 1
            src_, col, icn = qtiles[kq]
            w = ICW[icn]
            if col < 2:
                tap = src_[:, ICOFF[icn]:ICOFF[icn] + w].rearrange("(hc p) i -> p hc i", p=128)
                rows, cols = 128, 32 * w
            else:
                tap = src_[ICOFF[icn]:ICOFF[icn] + w, :]
                rows, cols = w, 4096
            b = kq % 2
            qn, sname = ("sync", "lb0") if b == 0 else ("gpsimd", "qlb1")
            wait(qn, "ve", lb_free[b])
            if col < 2:
                ldt = dma(qn, lbuf[b][0:rows, 0:cols].rearrange("p (hc i) -> p hc i", i=w), tap, sname)
            else:
                ldt = dma(qn, lbuf[b][0:rows, 0:cols], tap, sname)
            wait("vector", sname, ldt)
            if kq == 0:
                wait("vector", "ard", btbl_ard)
                wait("vector", "pe", tr_done_pe[7])
            ha = tmpa[b][0:rows, 0:cols]
            dst = qout[b][0:rows, 0:cols]
            wait("vector", f"qs{b}", q_gd[b])
            dr("vector")
            op("vector", lambda e, rows=rows, cols=cols, col=col, b=b, ha=ha: e.tensor_scalar(
                out=ha, in0=lbuf[b][0:rows, 0:cols],
                scalar1=btbl[0:rows, 3 + col:4 + col], scalar2=None, op0=ALU.is_lt))
            dr("vector")
            q_ve[b] = op("vector", lambda e, rows=rows, cols=cols, col=col, b=b, dst=dst, ha=ha: e.scalar_tensor_tensor(
                out=dst, in0=lbuf[b][0:rows, 0:cols], scalar=btbl[0:rows, col:col + 1],
                in1=ha, op0=ALU.is_gt, op1=ALU.subtract), sem="ve")
            lb_free[b] = q_ve[b]
            scr = [qg_scr, qu_scr, qd_scr][col]
            wait("gpsimd", "ve", q_ve[b])
            q_gd[b] = dma("gpsimd", scr[icn, 0:rows, 0:cols], qout[b][0:rows, 0:cols], f"qs{b}")


        val_scope = ExitStack()
        val_all = val_scope.enter_context(nc.sbuf_tensor("val_all", [128, S], BF16, side="right"))
        pa = ExitStack()
        ptb = pa.enter_context(nc.sbuf_tensor("ptb", [128, 4096], BF16))
        dn1 = pa.enter_context(nc.sbuf_tensor("dn1", [1, 512], F32))
        dnr = pa.enter_context(nc.sbuf_tensor("dnr", [1, 512], F32))
        rec_b = pa.enter_context(nc.sbuf_tensor("rec_b", [128, 512], F32))
        ps = [pa.enter_context(nc.psum_tensor(f"ps{i}", [128, 1024], F32)) for i in range(2)]
        pav = pa.enter_context(nc.psum_tensor("pav", [128, 512], F32))
        pbc = pa.enter_context(nc.psum_tensor("pbc", [128, 512], F32))

        wait("tensor", "sc", pp_sc)
        wait("tensor", "ve", pp_ve)
        wait("tensor", "wkv", wo_ld)
        wait("scalar", "gp", gp_sum)
        wait("scalar", "ard", CNT["ard"])
        wait("sync", "sc", pp_sc)
        wait("sync", "ve", pp_ve)

        last_exp_sc = {0: 0, 1: 0}
        last_av_pe = {0: 0, 1: 0}
        wst_gd = {0: 0, 1: 0}
        vn_prev = 0
        rb_prev = 0
        for qb in range(NSB):
            q0 = qb * 512

            def emit_s(kc2, q0=q0):
                g = kc2 % 2
                wait("tensor", "sc", last_exp_sc[g])
                if kc2 == 0:
                    wait("tensor", "ve", vn_prev)
                    wait("tensor", "ve", rb_prev)
                for u in range(2):
                    kc = kc2 * 2 + u
                    op("tensor", lambda e, kc=kc, g=g, u=u, q0=q0: e.matmul(
                        ps[g][:, u * 512:(u + 1) * 512], kt_s[:, kc * 128:(kc + 1) * 128],
                        qt_s[:, q0:q0 + 512], start=True, stop=True, skip_group_check=True),
                       sem="pe" if u == 1 else None)
                smm = CNT["pe"]
                wait("scalar", "pe", smm)
                wait("scalar", "pe", last_av_pe[g])
                last_exp_sc[g] = op("scalar", lambda e, g=g: e.activation(
                    out=ptb[:, g * 2048:g * 2048 + 1024], in_=ps[g][:, :], func=AF.Exp), sem="sc")

            def emit_av(kc2):
                g = kc2 % 2
                wait("tensor", "sc", last_exp_sc[g])
                for u in range(2):
                    kc = kc2 * 2 + u
                    pslice = ptb[:, g * 2048 + u * 512: g * 2048 + (u + 1) * 512]
                    op("tensor", lambda e, kc=kc, pslice=pslice: e.matmul(
                        pav[:, :], v_s[:, kc * 128:(kc + 1) * 128], pslice,
                        start=(kc == 0), stop=(kc == NKC - 1), skip_group_check=True))
                    op("tensor", lambda e, kc=kc, pslice=pslice: e.matmul(
                        pbc[:, :], ones_ab[:, :], pslice,
                        start=(kc == 0), stop=(kc == NKC - 1), skip_group_check=True),
                       sem="pe" if u == 1 else None)
                last_av_pe[g] = CNT["pe"]

            emit_s(0)
            emit_s(1)
            emit_av(0)
            emit_s(2)
            emit_av(1)
            for kc2 in range(3, NKC // 2):
                emit_s(kc2)
                emit_av(kc2 - 1)
            emit_av(NKC // 2 - 1)
            av_last = CNT["pe"]

            wait("vector", "pe", av_last)
            if "dn" in tap_aps:
                dn_sc = op("scalar", lambda e: e.copy(out=dn1[0:1, :], in_=pbc[0:1, :]), sem="sc")
                wait("sync", "sc", dn_sc)
                dma("sync", tap_aps["dn"][0:1, q0:q0 + 512], dn1[0:1, :], "st")
                wait("vector", "st", CNT["st"])
            rb_prev = op("vector", lambda e: e.reciprocal(out=rec_b[:, :], in_=pbc[:, :]), sem="ve")
            wait("vector", "ve", rb_prev)
            vn_prev = op("vector", lambda e, q0=q0: e.tensor_tensor(
                out=val_all[:, q0:q0 + 512], in0=pav[:, :], in1=rec_b[:, :], op=ALU.mult), sem="ve")
            if "valn" in tap_aps:
                wait("sync", "ve", vn_prev)
                dma("sync", tap_aps["valn"][:, q0:q0 + 512], val_all[:, q0:q0 + 512], "st")
            for _ in range(3):
                emit_quant_tile()
        while qstate["kq"] < len(qtiles):
            emit_quant_tile()
        quant_ve = max(q_ve.values())
        quant_qs = dict(q_gd)
        val_done_ve = CNT["ve"]
        pa_sc = CNT["sc"]
        pa_ve = CNT["ve"]
        pa.close()
        p0.close()

        wdt_scope = ExitStack()
        wdt_q = wdt_scope.enter_context(nc.sbuf_tensor("wdt_q", [128, NIC * 4096], BF16))
        wait("sync", "qs0", quant_qs[0])
        wait("sync", "qs1", quant_qs[1])
        wait("sync", "sc", pa_sc)
        wait("sync", "ve", pa_ve)
        wdt_ld = 0
        for icn in range(NIC):
            w = ICW[icn]
            wdt_ld = dma("sync", wdt_q[0:w, icn * 4096:(icn + 1) * 4096], qd_scr[icn, 0:w, :], "wdl")

        wo_scope = ExitStack()
        wst = [wo_scope.enter_context(nc.sbuf_tensor(f"wst{i}", [128, 4096], BF16)) for i in range(2)]
        pwo = [wo_scope.enter_context(nc.psum_tensor(f"pwo{i}", [128, 512], F32)) for i in range(2)]
        wo_ev_p = {0: 0, 1: 0}
        wait("tensor", "ve", val_done_ve)
        wait("scalar", "sc", pa_sc)
        wait("scalar", "ve", pa_ve)
        wait("scalar", "qs0", quant_qs[0])
        wait("scalar", "qs1", quant_qs[1])
        for qb in range(NSB):
            m = qb // 2
            for j in range(4):
                r0 = (qb % 2) * 512 + j * 128
                bj = j % 2
                for h8 in range(8):
                    bw = h8 % 2
                    wait("tensor", "sc", wo_ev_p[bw])
                    op("tensor", lambda e, qb=qb, j=j, h8=h8, bw=bw: e.matmul(
                        pwo[bw][:, :], val_all[:, qb * 512 + j * 128: qb * 512 + (j + 1) * 128],
                        wo_sl[:, h8 * 512:(h8 + 1) * 512],
                        start=True, stop=True, skip_group_check=True), sem="pe")
                    wo_pe = CNT["pe"]
                    wait("scalar", "pe", wo_pe)
                    if h8 < 2:
                        wait("scalar", f"ws{bj}", wst_gd[bj])
                    wo_ev_p[bw] = op("scalar", lambda e, wb=wst[bj], h8=h8, bw=bw: e.copy(
                        out=wb[:, h8 * 512:(h8 + 1) * 512], in_=pwo[bw][:, :]), sem="sc")
                wait("gpsimd", "sc", wo_ev_p[0])
                wait("gpsimd", "sc", wo_ev_p[1])
                wst_gd[bj] = dma("gpsimd", rs_in[m, r0:r0 + 128, :], wst[bj][:, :], f"ws{bj}")
        wo_dma_all = dict(wst_gd)
        wo_scope.close()
        val_scope.close()
        qkv_scope.close()

        pf = ExitStack()
        xn_sb = pf.enter_context(nc.sbuf_tensor("xn_sb", [128, 32 * 512], BF16))
        interT = pf.enter_context(nc.sbuf_tensor("interT", [128, NIC * 512], BF16))
        gw = [pf.enter_context(nc.sbuf_tensor(f"gw{i}", [128, 4096], BF16)) for i in range(3)]
        gts = [pf.enter_context(nc.sbuf_tensor(f"gts{i}", [128, 512], BF16)) for i in range(2)]
        fst = [pf.enter_context(nc.sbuf_tensor(f"fst{i}", [128, 1024], BF16)) for i in range(2)]
        pg = pf.enter_context(nc.psum_tensor("pg", [128, 512], F32))
        pu = pf.enter_context(nc.psum_tensor("pu", [128, 512], F32))
        pd = [pf.enter_context(nc.psum_tensor(f"pd{i}", [128, 1024], F32)) for i in range(2)]

        wait("tensor", "sc", pa_sc)
        wait("tensor", "ve", pa_ve)
        wait("sync", "sc", pa_sc)
        wait("sync", "ve", pa_ve)
        wait("sync", "ag", ag_done)
        wait("sync", "qs0", quant_qs[0])
        wait("sync", "qs1", quant_qs[1])
        wait("gpsimd", "ws0", wo_dma_all[0])
        wait("gpsimd", "ws1", wo_dma_all[1])
        wait("sync", "ws0", wo_dma_all[0])
        wait("sync", "ws1", wo_dma_all[1])

        wait("scalar", "ard", btbl_ard)

        def xn_src1(sb):
            r, cs = sb // 2, (sb % 2) * 512
            return ag_o1[r * 2048:(r + 1) * 2048, cs:cs + 512].rearrange("(hc p) q -> p hc q", p=128)

        def xn_src2(sb):
            r, cs = sb // 2, (sb % 2) * 512
            return ag_o2[r * 2048:(r + 1) * 2048, cs:cs + 512].rearrange("(hc p) q -> p hc q", p=128)

        xn_dst1 = xn_sb.ap()[:, 0:16 * 512].rearrange("p (hc q) -> p hc q", q=512)
        xn_dst2 = xn_sb.ap()[:, 16 * 512:32 * 512].rearrange("p (hc q) -> p hc q", q=512)

        def gbuf(sb, icn, which):
            return (2 * (sb * NIC + icn) + which) % 3

        GWS = ["gwa", "gwb", "gwc"]
        dma("sync", xn_dst1, xn_src1(0), "xns")
        xn_ld = {0: dma("sync", xn_dst2, xn_src2(0), "xns")}
        gw_ld = {}
        gw_pe = {0: 0, 1: 0, 2: 0}
        gw_ld[(0, 0, 0)] = dma("sync", gw[0][:, 0:4096], qg_scr[0, :, :], "gwa")
        gw_ld[(0, 0, 1)] = dma("sync", gw[1][:, 0:4096], qu_scr[0, :, :], "gwb")

        g_ev = 0
        i_ve = 0
        down_pe_done = 0
        d_ev = {0: 0, 1: 0}
        f_dma = {0: 0, 1: 0}
        gts_free = {0: 0, 1: 0}
        rs_launched = {}

        for sb in range(NSB):
            m, r0 = sb // 2, (sb % 2) * 512
            for icn in range(NIC):
                w = ICW[icn]
                gb, ub = gbuf(sb, icn, 0), gbuf(sb, icn, 1)
                wait("tensor", GWS[gb], gw_ld[(sb, icn, 0)])
                if icn == 0:
                    wait("tensor", "xns", xn_ld[sb])
                wait("tensor", "sc", g_ev)
                for hc in range(32):
                    op("tensor", lambda e, hc=hc, w=w, gb=gb: e.matmul(
                        pg[0:w, :], gw[gb][:, hc * w:(hc + 1) * w], xn_sb[:, hc * 512:(hc + 1) * 512],
                        start=(hc == 0), stop=(hc == 31)), sem="pe" if hc == 31 else None)
                g_mm = CNT["pe"]
                gw_pe[gb] = g_mm
                wait("tensor", GWS[ub], gw_ld[(sb, icn, 1)])
                wait("tensor", "ve", i_ve)
                for hc in range(32):
                    op("tensor", lambda e, hc=hc, w=w, ub=ub: e.matmul(
                        pu[0:w, :], gw[ub][:, hc * w:(hc + 1) * w], xn_sb[:, hc * 512:(hc + 1) * 512],
                        start=(hc == 0), stop=(hc == 31)), sem="pe" if hc == 31 else None)
                u_mm = CNT["pe"]
                gw_pe[ub] = u_mm
                nsb, nic = (sb, icn + 1) if icn + 1 < NIC else (sb + 1, 0)
                if nsb < NSB:
                    for which in range(2):
                        nb = gbuf(nsb, nic, which)
                        wait("sync", "pe", gw_pe[nb])
                        scr = qg_scr if which == 0 else qu_scr
                        cols = 32 * ICW[nic]
                        gw_ld[(nsb, nic, which)] = dma("sync", gw[nb][:, 0:cols], scr[nic, 0:128, 0:cols], GWS[nb])
                bg = icn % 2
                wait("scalar", "pe", g_mm)
                wait("scalar", "ve", gts_free[bg])
                g_ev = op("scalar", lambda e, w=w, b2=gts[bg]: e.copy(out=b2[0:w, :], in_=pg[0:w, :]), sem="sc")
                wait("vector", "sc", g_ev)
                wait("vector", "pe", u_mm)
                if sb > 0 and icn == 0:
                    wait("vector", "pe", down_pe_done)
                i_ve = op("vector", lambda e, w=w, icn=icn, b2=gts[bg]: e.tensor_tensor(
                    out=interT[0:w, icn * 512:(icn + 1) * 512], in0=b2[0:w, :], in1=pu[0:w, :],
                    op=ALU.mult), sem="ve")
                gts_free[bg] = i_ve
            gu_done_pe = CNT["pe"]
            inter_done_ve = CNT["ve"]
            if sb + 1 < NSB:
                wait("sync", "pe", gu_done_pe)
                dma("sync", xn_dst1, xn_src1(sb + 1), "xns")
                xn_ld[sb + 1] = dma("sync", xn_dst2, xn_src2(sb + 1), "xns")
            if sb == 0:
                wait("tensor", "wdl", wdt_ld)
            wait("tensor", "ve", inter_done_ve)
            kk = 0
            for qt in range(4):
                for quar in range(4):
                    pb = pd[kk % 2]
                    wait("tensor", "sc", d_ev[kk % 2])
                    for icn in range(NIC):
                        w = ICW[icn]
                        lhs = interT[0:w, icn * 512 + qt * 128: icn * 512 + (qt + 1) * 128]
                        for u in range(2):
                            op("tensor", lambda e, w=w, icn=icn, lhs=lhs, quar=quar, u=u, pb=pb: e.matmul(
                                pb[:, u * 512:(u + 1) * 512], lhs,
                                wdt_q[0:w, icn * 4096 + quar * 1024 + u * 512: icn * 4096 + quar * 1024 + (u + 1) * 512],
                                start=(icn == 0), stop=(icn == NIC - 1)),
                               sem="pe" if (u == 1 and icn == NIC - 1) else None)
                    d_mm = CNT["pe"]
                    fb = fst[kk % 2]
                    wait("scalar", "pe", d_mm)
                    wait("scalar", f"fd{kk % 2}", f_dma[kk % 2])
                    d_ev[kk % 2] = op("scalar", lambda e, fb=fb, pb=pb: e.activation(
                        out=fb[:, :], in_=pb[:, :], func=AF.Copy, scale=btbl[:, 6:7]), sem="sc")
                    wait("gpsimd", "sc", d_ev[kk % 2])
                    f_dma[kk % 2] = dma("gpsimd",
                                        rs_in[m, r0 + qt * 128: r0 + (qt + 1) * 128, quar * 1024:(quar + 1) * 1024],
                                        fb[:, :], f"fd{kk % 2}", accum_op=ALU.add)
                    kk += 1
            down_pe_done = CNT["pe"]
            if sb % 2 == 1:
                wait("gpsimd", "fd0", CNT["fd0"])
                wait("gpsimd", "fd1", CNT["fd1"])
                rs_launched[m] = coll("ReduceScatter", ALU.add,
                                      [rs_in[m].opt()], [rs_o[m].opt()], f"rs{m}")
        if "interT" in tap_aps:
            wait("sync", "ve", CNT["ve"])
            dma("sync", tap_aps["interT"], interT[:, :], "st")
        if "rsin0" in tap_aps:
            wait("sync", "fd0", CNT["fd0"])
            wait("sync", "fd1", CNT["fd1"])
            dma("sync", tap_aps["rsin0"], rs_in[0], "st")
        pf_sc = CNT["sc"]
        pf_pe = CNT["pe"]
        pf.close()
        wdt_scope.close()

        fin = ExitStack()
        fina = [fin.enter_context(nc.sbuf_tensor(f"fina{i}", [128, 4096], BF16)) for i in range(2)]
        finb = [fin.enter_context(nc.sbuf_tensor(f"finb{i}", [128, 4096], F32)) for i in range(2)]
        fino = [fin.enter_context(nc.sbuf_tensor(f"fino{i}", [128, 4096], F32)) for i in range(2)]
        wait("sync", "sc", pf_sc)
        wait("sync", "pe", pf_pe)
        wait("vector", "fd0", CNT["fd0"])
        wait("vector", "fd1", CNT["fd1"])
        add_ve = {0: 0, 1: 0}
        out_st = {0: 0, 1: 0}
        for m in range(8):
            b = m % 2
            wait("sync", f"rs{m}", rs_launched[m])
            wait("sync", "ve", add_ve[b])
            if out_st[b]:
                wait("sync", "st", CNT["st"])
            la = dma("sync", fina[b][:, :], rs_o[m], f"ff{b}")
            lb2 = dma("sync", finb[b][:, :], x_res[m * 128:(m + 1) * 128, :], f"ff{b}")
            wait("vector", f"ff{b}", lb2)
            add_ve[b] = op("vector", lambda e, b=b: e.tensor_tensor(
                out=fino[b][:, :], in0=fina[b][:, :], in1=finb[b][:, :], op=ALU.add), sem="ve")
            wait("gpsimd", "ve", add_ve[b])
            out_st[b] = dma("gpsimd", out_c[m * 128:(m + 1) * 128, :], fino[b][:, :], "st")
        wait("sync", "st", CNT["st"])
        fin.close()

        block = ec(nc.Block())

        @block.sync
        def _(eng):
            for e, fn in PROG:
                if e == "sync":
                    fn(eng)

        @block.gpsimd
        def _(eng):
            for e, fn in PROG:
                if e == "gpsimd":
                    fn(eng)

        @block.vector
        def _(eng):
            for e, fn in PROG:
                if e == "vector":
                    fn(eng)

        @block.scalar
        def _(eng):
            for e, fn in PROG:
                if e == "scalar":
                    fn(eng)

        @block.tensor
        def _(eng):
            for e, fn in PROG:
                if e == "tensor":
                    fn(eng)

    return nc


def make_in_maps(L0_attn_x, L0_attn_wq, L0_attn_wk, L0_attn_wv, L0_attn_wo,
                 L0_ffn_w_gate, L0_ffn_w_up, L0_ffn_w_down):
    x2 = np.ascontiguousarray(np.asarray(L0_attn_x, dtype=np.float32).reshape(S, H))
    wq = np.asarray(L0_attn_wq, dtype=np.float32)
    wk = np.asarray(L0_attn_wk, dtype=np.float32)
    wv = np.asarray(L0_attn_wv, dtype=np.float32)
    wo = np.asarray(L0_attn_wo, dtype=np.float32)
    wgT_f = np.ascontiguousarray(np.asarray(L0_ffn_w_gate, dtype=np.float32).T)
    wuT_f = np.ascontiguousarray(np.asarray(L0_ffn_w_up, dtype=np.float32).T)
    wdT_f = np.ascontiguousarray(np.asarray(L0_ffn_w_down, dtype=np.float32).T)
    in_maps = []
    for c in range(NCORE):
        xr = np.concatenate([x2[m * 1024 + c * 128: m * 1024 + (c + 1) * 128] for m in range(8)], axis=0)
        in_maps.append({
            "x_ln": np.ascontiguousarray(x2[c * SLOC:(c + 1) * SLOC]),
            "x_res": np.ascontiguousarray(xr),
            "wq4": np.ascontiguousarray(wq[:, c * 512:(c + 1) * 512]),
            "wk1": np.ascontiguousarray(wk[:, c * 128:(c + 1) * 128]),
            "wv1": np.ascontiguousarray(wv[:, c * 128:(c + 1) * 128]),
            "wo1": np.ascontiguousarray(wo[c * 128:(c + 1) * 128, :]),
            "wgT": np.ascontiguousarray(wgT_f[:, c * IPC:(c + 1) * IPC]),
            "wuT": np.ascontiguousarray(wuT_f[:, c * IPC:(c + 1) * IPC]),
            "wdT": np.ascontiguousarray(wdT_f[c * IPC:(c + 1) * IPC, :]),
        })
    return in_maps


def assemble(results):
    out = np.empty((S, H), dtype=np.float32)
    for c in range(NCORE):
        oc = results[c]["out_c"]
        for m in range(8):
            out[m * 1024 + c * 128: m * 1024 + (c + 1) * 128] = oc[m * 128:(m + 1) * 128]
    return out.reshape(1, S, H)


_NC_CACHE = {}


def _get_nc(taps=()):
    key = tuple(sorted(taps))
    if key not in _NC_CACHE:
        _NC_CACHE[key] = build(taps=key)
    return _NC_CACHE[key]


def run(inputs, taps=()):
    nc = _get_nc(taps)
    in_maps = make_in_maps(**inputs)
    return run_bass_kernel_spmd(nc, in_maps, core_ids=list(range(NCORE)))


def kernel(**inputs):
    res = run(inputs)
    return assemble(res.results)
